# version 02
# baseline (speedup 1.0000x reference)
import math
from contextlib import ExitStack

import numpy as np
import ml_dtypes

import concourse.bass as bass
import concourse.mybir as mybir
from concourse.bass_utils import run_bass_kernel_spmd

F32 = mybir.dt.float32
BF16 = mybir.dt.bfloat16
AF = mybir.ActivationFunctionType
ALU = mybir.AluOpType
AX = mybir.AxisListType

NCORE = 8
R = 1024
L = 4096
NQ = 32
NKT = 17
KH = 2049
KHP = NKT * 128
RB = 512
NRB = R // RB
RTB = RB // 128
NRT = R // 128
EPS = 1e-5

_BF = ml_dtypes.bfloat16
_TABLES = None


def _make_tables():
    global _TABLES
    if _TABLES is not None:
        return _TABLES
    t = np.arange(L, dtype=np.float64)
    k = np.arange(KHP, dtype=np.float64)
    ang = 2.0 * np.pi / L
    kmask = (k <= 2048).astype(np.float64)
    wk = np.where((k == 0) | (k == 2048), 1.0, 2.0) * kmask

    phase = np.outer(t, k) * ang
    t1c = np.cos(phase) * kmask[None, :]
    t1s = np.sin(phase) * kmask[None, :]

    def fwd_layout(m):
        return np.ascontiguousarray(
            m.reshape(NQ, 128, NKT, 128).transpose(2, 1, 0, 3)).astype(_BF)

    tau = np.arange(KHP, dtype=np.float64)
    taumask = (tau <= 2048).astype(np.float64)
    t2 = np.cos(np.outer(k, tau) * ang) * (wk / L)[:, None] * taumask[None, :]

    def half_layout(m):
        return np.ascontiguousarray(
            m.reshape(NKT, 128, NKT, 128).transpose(2, 1, 0, 3)).astype(_BF)

    d = np.arange(KHP, dtype=np.float64)
    wd = np.where((d == 0) | (d >= 2049), 0.0, np.where(d == 2048, 1.0, 2.0))
    t3 = np.cos(np.outer(d, k) * ang) * wd[:, None] * kmask[None, :]

    t4base = np.outer(k, t) * ang
    t4c = np.cos(t4base) * (wk / L)[:, None]
    t4s = np.sin(t4base) * (wk / L)[:, None]

    def inv_layout(m):
        return np.ascontiguousarray(
            m.reshape(NKT, 128, NQ, 128).transpose(2, 1, 0, 3)).astype(_BF)

    _out = dict(
        t1c=fwd_layout(t1c), t1s=fwd_layout(t1s),
        t2c=half_layout(t2), t3c=half_layout(t3),
        t4c=inv_layout(t4c), t4s=inv_layout(t4s),
        identb=np.asarray(np.eye(128, dtype=np.float32).astype(_BF)),
        identf=np.eye(128, dtype=np.float32),
        onesb=np.asarray(np.ones((1, 128), np.float32).astype(_BF)),
        onesf=np.ones((1, 128), np.float32),
    )
    globals()["_TABLES"] = _out
    return _out


class Rec:

    def __init__(self):
        self.steps = []
        self.cnt = {}

    def now(self, sem):
        return (sem, self.cnt.get(sem, 0))

    def barrier(self):
        return [self.now(s) for s in self.cnt]

    def step(self, eng, fn, waits=(), inc=None, n=1):
        w = {}
        for tk in waits:
            if tk is None:
                continue
            s, v = tk
            w[s] = max(w.get(s, 0), v)
        tok = None
        if inc is not None:
            self.cnt[inc] = self.cnt.get(inc, 0) + n
            tok = (inc, self.cnt[inc])
        self.steps.append((eng, fn, sorted(w.items()), inc, n))
        return tok


def build_program():
    nc = bass.Bass("TRN2", target_bir_lowering=False)
    tb = _make_tables()

    d_x = nc.dram_tensor("x", [R, L], F32, kind="ExternalInput").ap()
    d_xT = nc.dram_tensor("xT", [L, R], F32, kind="ExternalInput").ap()
    d_w = nc.dram_tensor("w", [L, L], F32, kind="ExternalInput").ap()
    d_bias = nc.dram_tensor("biasr", [1, L], F32, kind="ExternalInput").ap()
    d_gate = nc.dram_tensor("gateb", [128, 16], F32, kind="ExternalInput").ap()
    d_lA = nc.dram_tensor("lA", [16, 128, 1024], F32, kind="ExternalInput").ap()
    d_lAs = nc.dram_tensor("lAs", [128, 1024], F32, kind="ExternalInput").ap()
    d_lB = nc.dram_tensor("lB", [16, 32, L], F32, kind="ExternalInput").ap()
    d_lBs = nc.dram_tensor("lBs", [32, L], F32, kind="ExternalInput").ap()
    d_t1c = nc.dram_tensor("t1c", list(tb["t1c"].shape), BF16, kind="ExternalInput").ap()
    d_t1s = nc.dram_tensor("t1s", list(tb["t1s"].shape), BF16, kind="ExternalInput").ap()
    d_t2c = nc.dram_tensor("t2c", list(tb["t2c"].shape), BF16, kind="ExternalInput").ap()
    d_t3c = nc.dram_tensor("t3c", list(tb["t3c"].shape), BF16, kind="ExternalInput").ap()
    d_t4c = nc.dram_tensor("t4c", list(tb["t4c"].shape), BF16, kind="ExternalInput").ap()
    d_t4s = nc.dram_tensor("t4s", list(tb["t4s"].shape), BF16, kind="ExternalInput").ap()
    d_idb = nc.dram_tensor("identb", [128, 128], BF16, kind="ExternalInput").ap()
    d_idf = nc.dram_tensor("identf", [128, 128], F32, kind="ExternalInput").ap()
    d_onb = nc.dram_tensor("onesb", [1, 128], BF16, kind="ExternalInput").ap()
    d_onf = nc.dram_tensor("onesf", [1, 128], F32, kind="ExternalInput").ap()

    d_out = nc.dram_tensor("out", [R, L], F32, kind="ExternalOutput").ap()
    d_yscr = nc.dram_tensor("y_scr", [NRT, 128, L], F32).ap()
    d_zscr = nc.dram_tensor("z_scr", [NQ, 128, R], BF16).ap()
    d_fscr = nc.dram_tensor("f_scr", [NRT, 128, L], F32).ap()

    rec = Rec()

    def dma(eng, sem, out, in_, waits=()):
        return rec.step(eng, lambda e, o=out, i=in_: e.dma_start(out=o, in_=i),
                        waits=waits, inc=sem, n=16)

    def pe(fn, waits=()):
        return rec.step("tensor", fn, waits=waits, inc="pe")

    def ve(fn, waits=()):
        return rec.step("vector", fn, waits=waits, inc="ve")

    def se(fn, waits=()):
        return rec.step("scalar", fn, waits=waits, inc="se")

    with ExitStack() as ctx:
        ee = ctx.enter_context

        s_idb = ee(nc.sbuf_tensor("s_idb", [128, 128], BF16))
        s_idf = ee(nc.sbuf_tensor("s_idf", [128, 128], F32))
        s_onb = ee(nc.sbuf_tensor("s_onb", [1, 128], BF16))
        s_onf = ee(nc.sbuf_tensor("s_onf", [1, 128], F32))
        s_gate = ee(nc.sbuf_tensor("s_gate", [128, 16], F32))
        stat1 = ee(nc.sbuf_tensor("stat1", [128, 64], F32))
        stat2 = ee(nc.sbuf_tensor("stat2", [128, 64], F32))
        stat3 = ee(nc.sbuf_tensor("stat3", [128, 64], F32))
        stat4 = ee(nc.sbuf_tensor("stat4", [128, 64], F32))
        fsum = ee(nc.sbuf_tensor("fsum", [128, 64], F32))
        fsq = ee(nc.sbuf_tensor("fsq", [128, 64], F32))
        mcol = ee(nc.sbuf_tensor("mcol", [128, 16], F32))
        m8ab = ee(nc.sbuf_tensor("m8ab", [128, 16], F32))
        m8all = ee(nc.sbuf_tensor("m8all", [128, 32], F32))
        rstdrow = ee(nc.sbuf_tensor("rstdrow", [1, R], F32))
        a_comb = ee(nc.sbuf_tensor("a_comb", [128, 2048], BF16))
        b_comb = ee(nc.sbuf_tensor("b_comb", [64, L], BF16))
        s_ax = ee(nc.sbuf_tensor("s_ax", [64, 1024], BF16))

        sem_names = ["pe", "ve", "se", "dtab0", "dtab1", "dtabS0", "dtabS1",
                     "dx0", "dx1", "dxt", "dw0", "dw1", "dz0", "dz1", "dzw",
                     "dyw", "dyr", "dfw", "dfr", "dout", "dmisc",
                     "dlA0", "dlA1", "dlB0", "dlB1", "dbias"]
        sems = {s: ee(nc.semaphore(s)) for s in sem_names}

        dma("sync", "dmisc", s_idb[:], d_idb)
        dma("sync", "dmisc", s_idf[:], d_idf)
        dma("sync", "dmisc", s_onb[:], d_onb)
        dma("sync", "dmisc", s_onf[:], d_onf)
        tok_misc = dma("sync", "dmisc", s_gate[:], d_gate)

        with nc.sbuf_tensor("la_st", [128, 2048], F32) as la_st, \
             nc.sbuf_tensor("aeff", [128, 2048], F32) as aeff, \
             nc.sbuf_tensor("lb_st", [32, 2 * L], F32) as lb_st, \
             nc.sbuf_tensor("beff", [32, 2 * L], F32) as beff:
            la_v = [la_st[:, 0:1024], la_st[:, 1024:2048]]
            ae_v = [aeff[:, 0:1024], aeff[:, 1024:2048]]
            lb_v = [lb_st[:, 0:L], lb_st[:, L:2 * L]]
            be_v = [beff[:, 0:L], beff[:, L:2 * L]]
            tA = [None, None]
            tB = [None, None]
            last_a = None
            last_b = None
            pa_use = [None, None]
            pb_use = [None, None]
            for n in range(16):
                p = n % 2
                tA[p] = dma("sync", "dlA%d" % p, la_v[p], d_lA[n], waits=[pa_use[p]])
                tB[p] = dma("sync", "dlB%d" % p, lb_v[p], d_lB[n], waits=[pb_use[p]])
                gA = s_gate[:, n:n + 1]
                gB = s_gate[0:32, n:n + 1]
                dst_a = ae_v[(n + 1) % 2]
                src_a = ae_v[n % 2]
                if n == 0:
                    last_a = ve(lambda e, d0=dst_a, s0=la_v[p], g=gA:
                                e.tensor_scalar(d0, s0, g, None, op0=ALU.mult),
                                waits=[tA[p], tok_misc])
                else:
                    last_a = ve(lambda e, d0=dst_a, s0=la_v[p], g=gA, acc=src_a:
                                e.scalar_tensor_tensor(d0, s0, g, acc,
                                                       op0=ALU.mult, op1=ALU.add),
                                waits=[tA[p], last_a])
                pa_use[p] = last_a
                dst_b = be_v[(n + 1) % 2]
                src_b = be_v[n % 2]
                if n == 0:
                    last_b = ve(lambda e, d0=dst_b, s0=lb_v[p], g=gB:
                                e.tensor_scalar(d0, s0, g, None, op0=ALU.mult),
                                waits=[tB[p], tok_misc])
                else:
                    last_b = ve(lambda e, d0=dst_b, s0=lb_v[p], g=gB, acc=src_b:
                                e.scalar_tensor_tensor(d0, s0, g, acc,
                                                       op0=ALU.mult, op1=ALU.add),
                                waits=[tB[p], last_b])
                pb_use[p] = last_b
            ae_fin = ae_v[0]
            be_fin = be_v[0]
            ac3 = a_comb.ap().rearrange("p (q r) -> p q r", r=64)
            ae3 = ae_fin.rearrange("p (q r) -> p q r", r=32)
            t_ac0 = ve(lambda e: e.tensor_copy(ac3[:, :, 0:32], ae3), waits=[last_a])
            tAs = dma("sync", "dlA1", la_v[1], d_lAs, waits=[last_a])
            las3 = la_v[1].rearrange("p (q r) -> p q r", r=32)
            tok_acomb = ve(lambda e: e.tensor_copy(ac3[:, :, 32:64], las3),
                           waits=[tAs, t_ac0])
            t_bc0 = ve(lambda e: e.tensor_copy(b_comb[0:32, :], be_fin), waits=[last_b])
            tBs = dma("sync", "dlB1", lb_v[1], d_lBs, waits=[last_b])
            tok_bcomb = ve(lambda e: e.tensor_copy(b_comb[32:64, :], lb_v[1]),
                           waits=[tBs, t_bc0])

        def ln_stats(stat, base, tsum, tsq):
            c = lambda j: stat[:, base + j:base + j + 1]
            t1 = ve(lambda e: e.tensor_scalar(c(2), c(0), 1.0 / L, None, op0=ALU.mult),
                    waits=[tsum])
            t2 = ve(lambda e: e.tensor_scalar(c(3), c(1), 1.0 / L, None, op0=ALU.mult),
                    waits=[tsq])
            t3 = ve(lambda e: e.tensor_tensor(c(4), c(2), c(2), op=ALU.mult), waits=[t1])
            t4 = ve(lambda e: e.tensor_tensor(c(4), c(3), c(4), op=ALU.subtract),
                    waits=[t2, t3])
            t5 = ve(lambda e: e.tensor_scalar(c(4), c(4), EPS, None, op0=ALU.add),
                    waits=[t4])
            t6 = ve(lambda e: e.reciprocal(c(3), c(4)), waits=[t5])
            t7 = se(lambda e: e.activation(c(5), c(3), AF.Sqrt), waits=[t6])
            t8 = ve(lambda e: e.tensor_scalar(c(6), c(2), -1.0, c(5),
                                              op0=ALU.mult, op1=ALU.mult), waits=[t7])
            return t8

        tok_stats = [None] * NRT
        tok_rrow = [None] * NRT
        with nc.sbuf_tensor("p1x", [128, 2 * L], F32) as p1x, \
             nc.sbuf_tensor("trash", [128, L], BF16) as trash, \
             nc.psum_tensor("psA", [128, 512], F32) as psA:
            xv = [p1x[:, 0:L], p1x[:, L:2 * L]]
            tld = [None, None]
            tuse = [None, None]
            tok_psA = None
            for rt in range(NRT):
                p = rt % 2
                tld[p] = dma("sync", "dx%d" % p, xv[p],
                             d_x[rt * 128:(rt + 1) * 128, :],
                             waits=[tuse[p]] if tuse[p]
                             else [tok_acomb, tok_bcomb])
                b8 = rt * 8
                ts1 = ve(lambda e, s0=xv[p], o=stat1[:, b8:b8 + 1]:
                         e.reduce_sum(o, s0, axis=AX.X), waits=[tld[p]])
                ts2 = se(lambda e, s0=xv[p], o=stat1[:, b8 + 1:b8 + 2]:
                         e.activation(trash[:], s0, AF.Square, accum_out=o),
                         waits=[tld[p], ts1])
                tuse[p] = ts2
                tok_stats[rt] = ln_stats(stat1, b8, ts1, ts2)
                tt = pe(lambda e, c=stat1[:, b8 + 5:b8 + 6]:
                        e.transpose(psA[0:1, 0:128], c, s_idf[:]),
                        waits=[tok_stats[rt], tok_misc, tok_psA])
                tok_rrow[rt] = ve(lambda e, rt=rt: e.tensor_copy(
                    rstdrow[0:1, rt * 128:(rt + 1) * 128], psA[0:1, 0:128]),
                    waits=[tt])
                tok_psA = tok_rrow[rt]
        tok_allstats = [t for t in tok_stats + tok_rrow]

        tok_prev_block = None
        tok_zwr = []
        tok_ywr = []

        for b in range(NRB):
            rows = slice(b * RB, (b + 1) * RB)
            entry = (([tok_acomb, tok_bcomb] + tok_allstats) if b == 0
                     else list(tok_prev_block)) + rec.barrier()
            with ExitStack() as bctx:
                bb = bctx.enter_context
                vrtok = bb(nc.sbuf_tensor("vrtok_%d" % b, [128, RTB * L], BF16))
                bcst = bb(nc.sbuf_tensor("bcst_%d" % b, [128, RB], F32))
                stgc = bb(nc.sbuf_tensor("stgc_%d" % b, [128, 512], BF16))
                pmm = bb(nc.psum_tensor("pmm_%d" % b, [128, 512], F32))
                ptr = bb(nc.psum_tensor("ptr_%d" % b, [128, 1024], BF16))
                psB = bb(nc.psum_tensor("psB_%d" % b, [128, 512], F32))

                with ExitStack() as mctx:
                    mm_ = mctx.enter_context
                    ar45 = mm_(nc.sbuf_tensor("ar45_%d" % b, [128, 2 * NKT * RB], BF16))
                    u_r = mm_(nc.sbuf_tensor("u_r_%d" % b, [128, NKT * RB], BF16))
                    u_i = mm_(nc.sbuf_tensor("u_i_%d" % b, [128, NKT * RB], BF16))
                    slabP = mm_(nc.sbuf_tensor("slabP_%d" % b, [128, NKT * RB], BF16))
                    ctok = mm_(nc.sbuf_tensor("ctok_%d" % b, [128, RTB * KHP], BF16))
                    etok = mm_(nc.sbuf_tensor("etok_%d" % b, [128, RTB * KHP], BF16))
                    tabA = mm_(nc.sbuf_tensor("tabA_%d" % b, [128, 2 * L], BF16))
                    tabB = mm_(nc.sbuf_tensor("tabB_%d" % b, [128, 2 * KHP], BF16))
                    stgm = tabB[:, 0:2 * KHP]
                    stgf = mm_(nc.sbuf_tensor("stgf_%d" % b, [128, 1024], F32))

                    xt_v = ar45[:, 0:NQ * RB]
                    v_r = ar45[:, 0:NKT * RB]
                    v_i = ar45[:, NKT * RB:2 * NKT * RB]
                    tabAv = [tabA[:, 0:L], tabA[:, L:2 * L]]
                    tabB1 = tabB[:, 0:L]
                    tabB4 = [tabB[:, 0:KHP], tabB[:, KHP:2 * KHP]]

                    t_claim = ve(lambda e: e.memzero(ar45[:, NQ * RB:2 * NKT * RB]),
                                 waits=entry)
                    t_xt = None
                    for q in range(NQ):
                        t_xt = dma("gpsimd", "dxt", xt_v[:, q * RB:(q + 1) * RB],
                                   d_xT[q * 128:(q + 1) * 128, rows],
                                   waits=entry + [t_claim] if q == 0 else [])
                    t_k1 = pe(lambda e: e.matmul(psB[:], s_onf[:],
                                                 rstdrow[0:1, rows],
                                                 start=True, stop=True),
                              waits=tok_allstats + [tok_misc] + entry)
                    t_rb = ve(lambda e: e.tensor_copy(bcst[:], psB[:]), waits=[t_k1])

                    tok_tab = [None, None]
                    tok_tabu = [None, None]
                    tok_sinuse = None
                    tok_evac = None
                    tok_u_last = None
                    tok_P_last = None
                    for kt in range(NKT):
                        p = kt % 2
                        tok_tab[p] = dma("sync", "dtab%d" % p, tabAv[p],
                                         d_t1c[kt].rearrange("p q k -> p (q k)"),
                                         waits=[tok_tabu[p]] if tok_tabu[p]
                                         else entry)
                        tokb = dma("sync", "dtabS0", tabB1,
                                   d_t1s[kt].rearrange("p q k -> p (q k)"),
                                   waits=[tok_sinuse] if tok_sinuse else entry)
                        mm = None
                        for q in range(NQ):
                            mm = pe(lambda e, q=q, p=p: e.matmul(
                                pmm[:], tabAv[p][:, q * 128:(q + 1) * 128],
                                xt_v[:, q * RB:(q + 1) * RB],
                                start=(q == 0), stop=(q == NQ - 1)),
                                waits=[tok_tab[p], t_xt, tok_evac] if q == 0 else [])
                        sl = slice(kt * RB, (kt + 1) * RB)
                        e1 = ve(lambda e, sl=sl: e.tensor_tensor(
                            u_r[:, sl], pmm[:], bcst[:], op=ALU.mult),
                            waits=[mm, t_rb])
                        for q in range(NQ):
                            mm = pe(lambda e, q=q: e.matmul(
                                pmm[:], tabB1[:, q * 128:(q + 1) * 128],
                                xt_v[:, q * RB:(q + 1) * RB],
                                start=(q == 0), stop=(q == NQ - 1)),
                                waits=[tokb, e1] if q == 0 else [])
                        tok_tabu[p] = mm
                        tok_sinuse = mm
                        e2 = ve(lambda e, sl=sl: e.tensor_tensor(
                            u_i[:, sl], pmm[:], bcst[:], op=ALU.mult), waits=[mm])
                        q1 = se(lambda e, sl=sl: e.activation(
                            stgf[:, 0:512], u_r[:, sl], AF.Square), waits=[e1])
                        q2 = se(lambda e, sl=sl: e.activation(
                            stgf[:, 512:1024], u_i[:, sl], AF.Square), waits=[e2])
                        e3 = ve(lambda e, sl=sl: e.tensor_tensor(
                            slabP[:, sl], stgf[:, 0:512], stgf[:, 512:1024],
                            op=ALU.add), waits=[q1, q2])
                        if kt == 0:
                            e3 = ve(lambda e: e.memzero(slabP[0:1, 0:RB]), waits=[e3])
                        tok_evac = e2
                        tok_u_last = e2
                        tok_P_last = e3

                    tok_tab = [None, None]
                    tok_tabu = [tok_tabu[0], tok_tabu[1]]
                    tok_psfree = tok_u_last
                    tok_stgc_free = None
                    tok_ptr_free = None
                    tok_ct_last = None
                    for tt_ in range(NKT):
                        p = tt_ % 2
                        tok_tab[p] = dma("sync", "dtab%d" % p, tabAv[p][:, 0:KHP],
                                         d_t2c[tt_].rearrange("p q k -> p (q k)"),
                                         waits=[tok_tabu[p]])
                        mm = None
                        for kq in range(NKT):
                            mm = pe(lambda e, kq=kq, p=p: e.matmul(
                                pmm[:], tabAv[p][:, kq * 128:(kq + 1) * 128],
                                slabP[:, kq * RB:(kq + 1) * RB],
                                start=(kq == 0), stop=(kq == NKT - 1)),
                                waits=[tok_tab[p], tok_P_last, tok_psfree]
                                if kq == 0 else [])
                        tok_tabu[p] = mm
                        ec = ve(lambda e: e.tensor_copy(stgc[:], pmm[:]),
                                waits=[mm, tok_stgc_free])
                        tok_psfree = ec
                        ttr = None
                        for rq in range(RTB):
                            ttr = pe(lambda e, rq=rq: e.transpose(
                                ptr[:, rq * 128:(rq + 1) * 128],
                                stgc[:, rq * 128:(rq + 1) * 128], s_idb[:]),
                                waits=[ec, tok_ptr_free] if rq == 0 else [ec])
                        tok_stgc_free = ttr
                        ev = None
                        for rq in range(RTB):
                            ev = ve(lambda e, rq=rq, tt_=tt_: e.tensor_copy(
                                ctok[:, rq * KHP + tt_ * 128:
                                     rq * KHP + (tt_ + 1) * 128],
                                ptr[:, rq * 128:(rq + 1) * 128]), waits=[ttr])
                        tok_ptr_free = ev
                        tok_ct_last = ev

                    tok_et_last = None
                    for rq in range(RTB):
                        cv = ctok[:, rq * KHP:(rq + 1) * KHP]
                        evv = etok[:, rq * KHP:(rq + 1) * KHP]
                        t1_ = ve(lambda e, cv=cv: e.max(m8ab[:, 0:8], cv[:, 0:KH]),
                                 waits=[tok_ct_last])
                        t2_ = ve(lambda e, cv=cv: e.max(m8ab[:, 8:16], cv[:, 1:2048]),
                                 waits=[t1_])
                        t3_ = ve(lambda e, rq=rq: e.max(
                            m8all[:, rq * 8:(rq + 1) * 8], m8ab[:]), waits=[t2_])
                        c0 = m8all[:, rq * 8:rq * 8 + 1]
                        th = m8all[:, rq * 8 + 7:rq * 8 + 8]
                        t4_ = ve(lambda e, c0=c0, rq=rq: e.tensor_scalar(
                            mcol[:, rq:rq + 1], c0, -1.0, None, op0=ALU.mult),
                            waits=[t3_])
                        t5_ = ve(lambda e, cv=cv, th=th: e.tensor_scalar(
                            stgm[:, 0:KHP], cv, th, None, op0=ALU.is_ge),
                            waits=[t3_])
                        t6_ = se(lambda e, cv=cv, rq=rq: e.activation(
                            stgm[:, KHP:2 * KHP], cv, AF.Exp,
                            bias=mcol[:, rq:rq + 1]), waits=[t4_, t5_])
                        tok_et_last = ve(lambda e, evv=evv: e.tensor_tensor(
                            evv, stgm[:, KHP:2 * KHP], stgm[:, 0:KHP],
                            op=ALU.mult), waits=[t6_, t5_])

                    efm = ctok
                    tok_efm = None
                    for dt_ in range(NKT):
                        ttr = None
                        for rq in range(RTB):
                            ttr = pe(lambda e, rq=rq, dt_=dt_: e.transpose(
                                ptr[:, rq * 128:(rq + 1) * 128],
                                etok[:, rq * KHP + dt_ * 128:
                                     rq * KHP + (dt_ + 1) * 128], s_idb[:]),
                                waits=[tok_et_last, tok_ptr_free] if rq == 0 else [])
                        ev = None
                        for rq in range(RTB):
                            ev = ve(lambda e, rq=rq, dt_=dt_: e.tensor_copy(
                                efm[:, dt_ * RB + rq * 128:
                                    dt_ * RB + (rq + 1) * 128],
                                ptr[:, rq * 128:(rq + 1) * 128]), waits=[ttr])
                        tok_ptr_free = ev
                        tok_efm = ev

                    tok_tab = [None, None]
                    tok_psfree3 = tok_psfree
                    tok_v_last = None
                    for kt in range(NKT):
                        p = kt % 2
                        tok_tab[p] = dma("sync", "dtab%d" % p, tabAv[p][:, 0:KHP],
                                         d_t3c[kt].rearrange("p q k -> p (q k)"),
                                         waits=[tok_tabu[p]])
                        mm = None
                        for dq in range(NKT):
                            mm = pe(lambda e, dq=dq, p=p: e.matmul(
                                pmm[:], tabAv[p][:, dq * 128:(dq + 1) * 128],
                                efm[:, dq * RB:(dq + 1) * RB],
                                start=(dq == 0), stop=(dq == NKT - 1)),
                                waits=[tok_tab[p], tok_efm, tok_psfree3]
                                if dq == 0 else [])
                        tok_tabu[p] = mm
                        sl = slice(kt * RB, (kt + 1) * RB)
                        x1 = ve(lambda e, sl=sl: e.tensor_tensor(
                            v_r[:, sl], pmm[:], u_r[:, sl], op=ALU.mult), waits=[mm])
                        x2 = ve(lambda e, sl=sl: e.tensor_tensor(
                            v_i[:, sl], pmm[:], u_i[:, sl], op=ALU.mult), waits=[x1])
                        tok_psfree3 = x2
                        tok_v_last = x2

                    tok_tab = [None, None]
                    tok_psfree4 = tok_psfree3
                    tok_stgc_free = tok_stgc_free
                    tok_vr_last = None
                    for tt_ in range(NQ):
                        p = tt_ % 2
                        tok_tab[p] = dma("sync", "dtab%d" % p, tabAv[p][:, 0:KHP],
                                         d_t4c[tt_].rearrange("p q k -> p (q k)"),
                                         waits=[tok_tabu[p]])
                        tokb = dma("sync", "dtabS%d" % p, tabB4[p],
                                   d_t4s[tt_].rearrange("p q k -> p (q k)"),
                                   waits=[tok_tabu[p],
                                          tok_et_last if tt_ < 2 else None])
                        mm = None
                        for kq in range(NKT):
                            mm = pe(lambda e, kq=kq, p=p: e.matmul(
                                pmm[:], tabAv[p][:, kq * 128:(kq + 1) * 128],
                                v_r[:, kq * RB:(kq + 1) * RB],
                                start=(kq == 0), stop=False),
                                waits=[tok_tab[p], tok_v_last, tok_psfree4]
                                if kq == 0 else [])
                        for kq in range(NKT):
                            mm = pe(lambda e, kq=kq, p=p: e.matmul(
                                pmm[:], tabB4[p][:, kq * 128:(kq + 1) * 128],
                                v_i[:, kq * RB:(kq + 1) * RB],
                                start=False, stop=(kq == NKT - 1)),
                                waits=[tokb] if kq == 0 else [])
                        tok_tabu[p] = mm
                        ec = ve(lambda e: e.tensor_copy(stgc[:], pmm[:]),
                                waits=[mm, tok_stgc_free])
                        tok_psfree4 = ec
                        ttr = None
                        for rq in range(RTB):
                            ttr = pe(lambda e, rq=rq: e.transpose(
                                ptr[:, rq * 128:(rq + 1) * 128],
                                stgc[:, rq * 128:(rq + 1) * 128], s_idb[:]),
                                waits=[ec, tok_ptr_free] if rq == 0 else [ec])
                        tok_stgc_free = ttr
                        ev = None
                        for rq in range(RTB):
                            ev = ve(lambda e, rq=rq, tt_=tt_: e.tensor_copy(
                                vrtok[:, rq * L + tt_ * 128:
                                      rq * L + (tt_ + 1) * 128],
                                ptr[:, rq * 128:(rq + 1) * 128]), waits=[ttr])
                        tok_ptr_free = ev
                        tok_vr_last = ev

                    tok_mixer_done = [tok_tabu[0], tok_tabu[1], tok_vr_last,
                                      tok_psfree4]

                with ExitStack() as actx:
                    aa = actx.enter_context
                    asmA = aa(nc.sbuf_tensor("asmA_%d" % b, [128, L], F32))
                    asmB = aa(nc.sbuf_tensor("asmB_%d" % b, [128, L], F32))
                    asmC = aa(nc.sbuf_tensor("asmC_%d" % b, [128, L], F32))
                    ztok = aa(nc.sbuf_tensor("ztok_%d" % b, [128, L], BF16))
                    zstg = aa(nc.sbuf_tensor("zstg_%d" % b, [128, L], BF16))
                    trash = aa(nc.sbuf_tensor("trashA_%d" % b, [128, L], BF16))

                    tok_asm_pe = None
                    tok_prevasm = None
                    for rq in range(RTB):
                        rt = b * RTB + rq
                        p = rt % 2
                        b8 = rt * 8
                        m1 = stat1[:, b8 + 2:b8 + 3]
                        r1 = stat1[:, b8 + 5:b8 + 6]
                        tx = dma("sync", "dx%d" % p, asmA[:],
                                 d_x[rt * 128:(rt + 1) * 128, :],
                                 waits=tok_mixer_done + [tok_prevasm])
                        a1 = ve(lambda e, m1=m1, r1=r1: e.tensor_scalar(
                            asmB[:], asmA[:], m1, r1,
                            op0=ALU.subtract, op1=ALU.mult),
                            waits=[tx] + tok_mixer_done)
                        a2 = ve(lambda e, rq=rq: e.tensor_tensor(
                            asmC[:], asmB[:], vrtok[:, rq * L:(rq + 1) * L],
                            op=ALU.add),
                            waits=[a1, tok_ywr[-1] if tok_ywr else None])
                        s1 = ve(lambda e, o=stat2[:, b8:b8 + 1]:
                                e.reduce_sum(o, asmC[:], axis=AX.X), waits=[a2])
                        s2 = se(lambda e, o=stat2[:, b8 + 1:b8 + 2]:
                                e.activation(trash[:], asmC[:], AF.Square,
                                             accum_out=o), waits=[a2, s1])
                        tst = ln_stats(stat2, b8, s1, s2)
                        a3 = ve(lambda e, m2=stat2[:, b8 + 2:b8 + 3],
                                r2=stat2[:, b8 + 5:b8 + 6]: e.tensor_scalar(
                            asmB[:], asmC[:], m2, r2,
                            op0=ALU.subtract, op1=ALU.mult), waits=[tst, s2])
                        a4 = ve(lambda e: e.tensor_tensor(
                            asmC[:], asmB[:], asmA[:], op=ALU.add), waits=[a3])
                        tok_ywr.append(dma("sync", "dyw", d_yscr[rt], asmC[:],
                                           waits=[a4]))
                        s3 = ve(lambda e, o=stat3[:, b8:b8 + 1]:
                                e.reduce_sum(o, asmC[:], axis=AX.X), waits=[a4])
                        s4 = se(lambda e, o=stat3[:, b8 + 1:b8 + 2]:
                                e.activation(trash[:], asmC[:], AF.Square,
                                             accum_out=o), waits=[a4, s3])
                        tst3 = ln_stats(stat3, b8, s3, s4)
                        az = se(lambda e, nm=stat3[:, b8 + 6:b8 + 7],
                                r3=stat3[:, b8 + 5:b8 + 6]: e.activation(
                            ztok[:], asmC[:], AF.Identity, bias=nm, scale=r3),
                            waits=[tst3, s4])
                        tok_zt_pe = None
                        tok_ptr_free2 = tok_asm_pe
                        ev = None
                        for qg in range(8):
                            ttr = None
                            for j in range(4):
                                q = qg * 4 + j
                                ttr = pe(lambda e, q=q, j=j: e.transpose(
                                    ptr[:, j * 128:(j + 1) * 128],
                                    ztok[:, q * 128:(q + 1) * 128], s_idb[:]),
                                    waits=[az, tok_ptr_free2] if j == 0 else [az])
                            ev = ve(lambda e, qg=qg: e.tensor_copy(
                                zstg[:, qg * 512:(qg + 1) * 512],
                                ptr[:, 0:512]),
                                waits=[ttr, tok_zwr[-1] if (qg == 0 and tok_zwr)
                                       else None])
                            tok_ptr_free2 = ev
                            tok_zt_pe = ttr
                        zdst = d_zscr[:, :, rt * 128:(rt + 1) * 128].transpose([1, 0, 2])
                        zsrc = zstg.ap().rearrange("p (q r) -> p q r", r=128)
                        tok_zwr.append(dma("sync", "dzw", zdst, zsrc, waits=[ev]))
                        tok_asm_pe = tok_zt_pe
                        tok_prevasm = ev

                tok_prev_block = [tok_asm_pe, tok_prevasm,
                                  tok_zwr[-1], tok_ywr[-1]]

        with nc.sbuf_tensor("zld", [128, 2048], BF16) as zld, \
             nc.psum_tensor("pax", [64, 1024], F32) as pax:
            zv = [zld[:, 0:1024], zld[:, 1024:2048]]
            tz = [None, None]
            tzu = [None, None]
            mm = None
            for q in range(NQ):
                p = q % 2
                tz[p] = dma("sync", "dz%d" % p, zv[p], d_zscr[q],
                            waits=[tzu[p]] +
                            (tok_zwr + tok_prev_block if q < 2 else []))
                for h in range(2):
                    mm = pe(lambda e, q=q, p=p, h=h: e.matmul(
                        pax[:, h * 512:(h + 1) * 512],
                        a_comb[:, q * 64:(q + 1) * 64],
                        zv[p][:, h * 512:(h + 1) * 512],
                        start=(q == 0), stop=(q == NQ - 1)),
                        waits=[tz[p], tok_acomb] + tok_prev_block
                        if q == 0 and h == 0 else ([tz[p]] if h == 0 else []))
                tzu[p] = mm
            tok_ax = ve(lambda e: e.tensor_copy(s_ax[:], pax[:]), waits=[mm])

        with ExitStack() as fctx:
            ff = fctx.enter_context
            biasb = ff(nc.sbuf_tensor("biasb", [128, L], F32))
            bstage = ff(nc.sbuf_tensor("bstage", [1, L], F32))
            wst = ff(nc.sbuf_tensor("wst", [128, 1024], BF16))
            zld2 = ff(nc.sbuf_tensor("zld2", [128, 2048], BF16))
            fstg = ff(nc.sbuf_tensor("fstg", [128, 512], F32))
            tr512 = ff(nc.sbuf_tensor("tr512", [128, 512], BF16))

            tok_bias = dma("sync", "dbias", bstage[:], d_bias,
                           waits=[tok_ax] + tok_prev_block)
            with nc.psum_tensor("psC", [128, 512], F32) as psC:
                tok_bb = None
                tpe = None
                for ot in range(8):
                    tpe = pe(lambda e, ot=ot: e.matmul(
                        psC[:], s_onf[:], bstage[0:1, ot * 512:(ot + 1) * 512],
                        start=True, stop=True),
                        waits=[tok_bias, tok_misc, tok_bb, tok_ax])
                    tok_bb = ve(lambda e, ot=ot: e.tensor_copy(
                        biasb[:, ot * 512:(ot + 1) * 512], psC[:]), waits=[tpe])

            with ExitStack() as pctx:
                pf = [pctx.enter_context(nc.psum_tensor("pf%d" % i, [128, 512], F32))
                      for i in range(8)]
                wv = [wst[:, 0:512], wst[:, 512:1024]]
                zv2 = [zld2[:, 0:1024], zld2[:, 1024:2048]]
                tw = [None, None]
                twu = [None, None]
                tz2 = [None, None]
                tz2u = [None, None]
                tok_fst = []
                tok_pf_free = [tok_bb] * 8
                tok_fstg_free = None
                for ot in range(8):
                    mm_rt = [None] * 8
                    for kq in range(NQ):
                        p = kq % 2
                        tw[p] = dma("gpsimd", "dw%d" % p, wv[p],
                                    d_w[kq * 128:(kq + 1) * 128,
                                        ot * 512:(ot + 1) * 512],
                                    waits=[twu[p]] if twu[p]
                                    else [tok_ax] + tok_prev_block)
                        tz2[p] = dma("sync", "dz%d" % p, zv2[p], d_zscr[kq],
                                     waits=[tz2u[p], tok_ax if (ot == 0 and kq < 2) else None]
                                     + (tok_zwr if ot == 0 and kq < 2 else []))
                        mm = None
                        for rt in range(8):
                            w0 = []
                            if rt == 0:
                                w0 = [tw[p], tz2[p]]
                                if kq == 0:
                                    w0 += [tok_pf_free[7], tok_ax]
                            mm = pe(lambda e, kq=kq, rt=rt, p=p, ot=ot: e.matmul(
                                pf[rt][:], zv2[p][:, rt * 128:(rt + 1) * 128],
                                wv[p], start=(kq == 0), stop=False), waits=w0)
                            mm_rt[rt] = mm
                        twu[p] = mm
                        tz2u[p] = mm
                    for rt in range(8):
                        mm_rt[rt] = pe(lambda e, rt=rt, ot=ot: e.matmul(
                            pf[rt][:], s_ax[:, rt * 128:(rt + 1) * 128],
                            b_comb[:, ot * 512:(ot + 1) * 512],
                            start=False, stop=True),
                            waits=[tok_ax, tok_bcomb] if rt == 0 else [])
                    for rt in range(8):
                        e1 = ve(lambda e, rt=rt, ot=ot: e.tensor_tensor(
                            fstg[:], pf[rt][:], biasb[:, ot * 512:(ot + 1) * 512],
                            op=ALU.add),
                            waits=[mm_rt[rt], tok_bb, tok_fstg_free])
                        tok_pf_free[rt] = e1
                        s1 = ve(lambda e, rt=rt, ot=ot: e.reduce_sum(
                            fsum[:, rt * 8 + ot:rt * 8 + ot + 1], fstg[:],
                            axis=AX.X), waits=[e1])
                        s2 = se(lambda e, rt=rt, ot=ot: e.activation(
                            tr512[:], fstg[:], AF.Square,
                            accum_out=fsq[:, rt * 8 + ot:rt * 8 + ot + 1]),
                            waits=[e1, s1])
                        tok_fw = dma("sync", "dfw", d_fscr[rt][:, ot * 512:(ot + 1) * 512],
                                     fstg[:], waits=[s1, s2])
                        tok_fst.append((s1, s2, tok_fw))
                        tok_fstg_free = tok_fw

                tok_fstats = [t for tpl in tok_fst for t in tpl[:2]]
                tok_fwr = [tpl[2] for tpl in tok_fst]

            with nc.sbuf_tensor("epA", [128, L], F32) as epA, \
                 nc.sbuf_tensor("epB", [128, L], F32) as epB, \
                 nc.sbuf_tensor("epY", [128, L], F32) as epY:
                tok_ep_use = None
                tok_out = []
                for rt in range(8):
                    b8 = rt * 8
                    c1 = ve(lambda e, rt=rt, o=stat4[:, b8:b8 + 1]: e.reduce_sum(
                        o, fsum[:, rt * 8:(rt + 1) * 8], axis=AX.X),
                        waits=tok_fstats)
                    c2 = ve(lambda e, rt=rt, o=stat4[:, b8 + 1:b8 + 2]: e.reduce_sum(
                        o, fsq[:, rt * 8:(rt + 1) * 8], axis=AX.X), waits=[c1])
                    tst4 = ln_stats(stat4, b8, c1, c2)
                    tf = dma("sync", "dfr", epA[:], d_fscr[rt],
                             waits=tok_fwr + [tok_ep_use,
                                              tok_out[-1] if tok_out else None])
                    ty = dma("sync", "dyr", epY[:], d_yscr[rt],
                             waits=tok_ywr + [tok_ep_use])
                    p1 = ve(lambda e, m4=stat4[:, b8 + 2:b8 + 3],
                            r4=stat4[:, b8 + 5:b8 + 6]: e.tensor_scalar(
                        epB[:], epA[:], m4, r4, op0=ALU.subtract, op1=ALU.mult),
                        waits=[tf, tst4])
                    p2 = ve(lambda e: e.tensor_tensor(epA[:], epB[:], epY[:],
                                                      op=ALU.add), waits=[p1, ty])
                    tok_ep_use = p2
                    tok_out.append(dma("sync", "dout", d_out[rt * 128:(rt + 1) * 128, :],
                                       epA[:], waits=[p2]))
                rec.step("sync", None, waits=tok_out)

        with nc.Block() as block:
            def mk(eng_name):
                def emit(eng):
                    for (e_, fn, waits, inc, n) in rec.steps:
                        if e_ != eng_name:
                            continue
                        for s, v in waits:
                            eng.wait_ge(sems[s], v)
                        if fn is None:
                            continue
                        inst = fn(eng)
                        if inc is not None:
                            inst.then_inc(sems[inc], n)
                return emit

            block.sync(mk("sync"))
            block.gpsimd(mk("gpsimd"))
            block.tensor(mk("tensor"))
            block.vector(mk("vector"))
            block.scalar(mk("scalar"))

    globals()["_LAST_REC"] = rec
    return nc


_NC_CACHE = None


def kernel(L0_attn_x, L0_ffn_gate, L0_ffn_kernel, L0_ffn_bias, L0_ffn_lora_A,
           L0_ffn_lora_B, L0_ffn_lora_A_shared, L0_ffn_lora_B_shared):
    global _NC_CACHE
    x = np.ascontiguousarray(np.asarray(L0_attn_x, dtype=np.float32).reshape(8192, L))
    gate = np.asarray(L0_ffn_gate, dtype=np.float32)
    w = np.ascontiguousarray(np.asarray(L0_ffn_kernel, dtype=np.float32))
    bias = np.asarray(L0_ffn_bias, dtype=np.float32).reshape(1, L)
    lora_A = np.asarray(L0_ffn_lora_A, dtype=np.float32)
    lora_B = np.asarray(L0_ffn_lora_B, dtype=np.float32)
    A_sh = np.asarray(L0_ffn_lora_A_shared, dtype=np.float32)
    B_sh = np.asarray(L0_ffn_lora_B_shared, dtype=np.float32)

    tb = _make_tables()

    lA = np.ascontiguousarray(
        lora_A.transpose(0, 2, 1).reshape(16, NQ, 128, 32)
        .transpose(0, 2, 1, 3).reshape(16, 128, 1024))
    lAs = np.ascontiguousarray(
        A_sh.transpose(1, 0).reshape(NQ, 128, 32).transpose(1, 0, 2)
        .reshape(128, 1024))
    lB = np.ascontiguousarray(lora_B.transpose(0, 2, 1))
    lBs = np.ascontiguousarray(B_sh.transpose(1, 0))

    if _NC_CACHE is None:
        _NC_CACHE = build_program()
    nc = _NC_CACHE

    in_maps = []
    for c in range(NCORE):
        shard = np.ascontiguousarray(x[c * R:(c + 1) * R])
        shard_T = np.ascontiguousarray(shard.T)
        gb = np.ascontiguousarray(np.tile(gate[c:c + 1, :], (128, 1)))
        in_maps.append({
            "x": shard, "xT": shard_T, "w": w, "biasr": bias, "gateb": gb,
            "lA": lA, "lAs": lAs, "lB": lB, "lBs": lBs,
            "t1c": tb["t1c"], "t1s": tb["t1s"], "t2c": tb["t2c"],
            "t3c": tb["t3c"], "t4c": tb["t4c"], "t4s": tb["t4s"],
            "identb": tb["identb"], "identf": tb["identf"],
            "onesb": tb["onesb"], "onesf": tb["onesf"],
        })

    res = run_bass_kernel_spmd(nc, in_maps, core_ids=list(range(NCORE)))
    out = np.concatenate([np.asarray(r["out"]) for r in res.results], axis=0)
    return out.reshape(2, 4096, L).astype(np.float32)
